# version 12
# baseline (speedup 1.0000x reference)
import math
import numpy as np
import ml_dtypes
import concourse.bass as bass
import concourse.mybir as mybir

F32 = mybir.dt.float32
BF16 = mybir.dt.bfloat16
AF = mybir.ActivationFunctionType
ALU = mybir.AluOpType
AX = mybir.AxisListType

ENGS = ("pe", "act", "dve", "pool", "sp")
DMA_SEMS = ("dcst", "dx0", "dx1", "dx2", "dx3", "dw0", "dw1", "dq0", "dq1",
            "dv0", "dsq", "dsk", "dsv0", "dsv1", "dxc", "dou", "dcp2", "dbst")
EXP_SHIFT = -30.0


class Sched:

    def __init__(self):
        self.q = {e: [] for e in ENGS}
        self.cnt = {}
        self.waited = {e: {} for e in ENGS}
        self.w = {}
        self.r = {}

    def _need(self, eng, tok):
        s, v = tok
        if eng == "pe" and s == "pe":
            return False
        return self.waited[eng].get(s, 0) < v

    def op(self, eng, fn, reads=(), writes=(), dma=None, label=""):
        deps = []
        for b in reads:
            t = self.w.get(b)
            if t:
                deps.append(t)
        for b in writes:
            t = self.w.get(b)
            if t:
                deps.append(t)
            for s, v in self.r.get(b, {}).items():
                deps.append((s, v))
        waits = []
        for t in deps:
            if self._need(eng, t):
                waits.append(t)
                self.waited[eng][t[0]] = max(self.waited[eng].get(t[0], 0), t[1])
        sem = dma if dma else eng
        amt = 16 if dma else 1
        self.cnt[sem] = self.cnt.get(sem, 0) + amt
        tok = (sem, self.cnt[sem])
        for b in writes:
            self.w[b] = tok
            self.r[b] = {}
        for b in reads:
            d = self.r.setdefault(b, {})
            if d.get(sem, 0) < tok[1]:
                d[sem] = tok[1]
        self.q[eng].append((waits, fn, sem, amt, label))
        return tok

    def alias(self, new, olds):
        toks = {}
        for o in olds:
            t = self.w.get(o)
            if t and toks.get(t[0], 0) < t[1]:
                toks[t[0]] = t[1]
            for s, v in self.r.get(o, {}).items():
                if toks.get(s, 0) < v:
                    toks[s] = v
        if toks:
            d = self.r.setdefault(new, {})
            for s, v in toks.items():
                if d.get(s, 0) < v:
                    d[s] = v

    def merge_writers(self, bufs):
        best = {}
        for b in bufs:
            t = self.w.get(b)
            if t and best.get(t[0], (0,))[0] < t[1]:
                best[t[0]] = (t[1],)
        assert len(best) <= 1, best
        for b in bufs:
            t = self.w.get(b)
            if t:
                self.w[b] = (t[0], best[t[0]][0])

    def emit(self, eng, proxy, sems):
        for waits, fn, sem, amt, label in self.q[eng]:
            for s, v in waits:
                proxy.wait_ge(sems[s], v)
            inst = fn(proxy)
            inst.then_inc(sems[sem], amt)


class Arena:

    def __init__(self, sched, tensor):
        self.sched = sched
        self.t = tensor
        self.occ = []

    def take(self, bufname, lo, hi):
        names = [bufname] if isinstance(bufname, str) else list(bufname)
        olds = [b for (l, h, bs) in self.occ if l < hi and h > lo for b in bs]
        if olds:
            for n in names:
                self.sched.alias(n, olds)
        newocc = []
        for (l, h, bs) in self.occ:
            if l < hi and h > lo:
                if l < lo:
                    newocc.append((l, lo, bs))
                if h > hi:
                    newocc.append((hi, h, bs))
            else:
                newocc.append((l, h, bs))
        newocc.append((lo, hi, names))
        self.occ = newocc
        return self.t[:, lo:hi]


def build_program(S=2048, D=4096):
    assert S % 512 == 0 and D % 256 == 0
    KC = D // 128
    NT = S // 128
    NB = S // 512
    NG = KC // 4
    assert KC % 4 == 0 and NT % 4 == 0
    SC = 1.0 / math.sqrt(128.0)
    HD2 = D // 2

    TBF_COLS = max(8 * S, 2 * D + S, S + D + 1024)

    nc = bass.Bass("TRN2", target_bir_lowering=False)

    x_d = nc.dram_tensor("x", [S, D], F32, kind="ExternalInput").ap()
    wq_d = nc.dram_tensor("wq", [D, D], F32, kind="ExternalInput").ap()
    wk_d = nc.dram_tensor("wk", [D, D], F32, kind="ExternalInput").ap()
    wv_d = nc.dram_tensor("wv", [D, D], F32, kind="ExternalInput").ap()
    wo_d = nc.dram_tensor("wo", [D, D], F32, kind="ExternalInput").ap()
    ap_d = nc.dram_tensor("ap", [128, D], F32, kind="ExternalInput").ap()
    bp_d = nc.dram_tensor("bp", [128, D], F32, kind="ExternalInput").ap()
    wrt_d = nc.dram_tensor("wrt", [D, 8], F32, kind="ExternalInput").ap()
    brs_d = nc.dram_tensor("brs_hbm", [1, 8], BF16, kind="ExternalInput").ap()
    ident_d = nc.dram_tensor("ident_hbm", [128, 128], BF16, kind="ExternalInput").ap()
    maskut_d = nc.dram_tensor("maskut_hbm", [128, 128], BF16, kind="ExternalInput").ap()
    onesd_d = nc.dram_tensor("onesd_hbm", [128, 128], BF16, kind="ExternalInput").ap()
    ones1_d = nc.dram_tensor("ones1_hbm", [1, 128], BF16, kind="ExternalInput").ap()
    emat_d = nc.dram_tensor("emat_hbm", [8, 128], BF16, kind="ExternalInput").ap()
    ctab_d = nc.dram_tensor("ctab_hbm", [128, S], BF16, kind="ExternalInput").ap()
    stab_d = nc.dram_tensor("stab_hbm", [128, S], BF16, kind="ExternalInput").ap()
    out_d = nc.dram_tensor("out", [S, D], F32, kind="ExternalOutput").ap()
    qts_d = nc.dram_tensor("qts", [D, S], BF16, kind="Internal").ap()
    kts_d = nc.dram_tensor("kts", [D, S], BF16, kind="Internal").ap()
    vs_d = nc.dram_tensor("vs", [KC, 128, S], BF16, kind="Internal").ap()

    sch = Sched()
    shuffle_mask = [i ^ 1 for i in range(32)]

    from contextlib import ExitStack
    with ExitStack() as _es:
        ec = _es.enter_context
        ident = ec(nc.sbuf_tensor("ident", [128, 128], BF16))
        maskut = ec(nc.sbuf_tensor("maskut", [128, 128], BF16))
        onesd = ec(nc.sbuf_tensor("onesd", [128, 128], BF16))
        ones1 = ec(nc.sbuf_tensor("ones1", [1, 128], BF16))
        emat = ec(nc.sbuf_tensor("emat", [8, 128], BF16))
        brs = ec(nc.sbuf_tensor("brs", [1, 8], BF16))
        xnt = ec(nc.sbuf_tensor("xnt", [128, KC * S], BF16))
        lst = ec(nc.sbuf_tensor("lst", [128, S], BF16))
        wpan0 = ec(nc.sbuf_tensor("wpan0", [128, D], BF16))
        wpan1 = ec(nc.sbuf_tensor("wpan1", [128, D], BF16))
        wrtb = ec(nc.sbuf_tensor("wrtb", [128, KC * 8], BF16))
        snb = ec(nc.sbuf_tensor("snb", [128, 16], BF16))
        scT = ec(nc.sbuf_tensor("scT", [8, S], BF16))
        rsc = ec(nc.sbuf_tensor("rsc", [128, 48], F32))
        st0 = ec(nc.sbuf_tensor("st0", [128, 64], F32))
        st1 = ec(nc.sbuf_tensor("st1", [128, 64], F32))
        kbias = ec(nc.sbuf_tensor("kbias", [128, 4], F32))
        tbf = ec(nc.sbuf_tensor("tbf", [128, TBF_COLS], BF16))
        tf32 = ec(nc.sbuf_tensor("tf32", [128, max(2 * S, D) + 1024], F32))
        psA = ec(nc.psum_tensor("psA", [128, 2048], F32))
        psB = ec(nc.psum_tensor("psB", [128, 2048], F32))
        sems = {}
        for s in ("pe", "act", "dve", "pool") + DMA_SEMS:
            sems[s] = ec(nc.semaphore(s))
        block = ec(nc.Block())
        wpan = [wpan0, wpan1]
        TB = Arena(sch, tbf)
        TF = Arena(sch, tf32)

        def xnt_sl(c, lo, hi):
            return xnt[:, c * S + lo: c * S + hi]

        for name, dst, src in [
            ("ident", ident, ident_d), ("maskut", maskut, maskut_d),
            ("onesd", onesd, onesd_d), ("ones1", ones1, ones1_d),
            ("emat", emat, emat_d), ("brs", brs, brs_d),
        ]:
            sch.op("sp", lambda e, d=dst, s=src: e.dma_start(out=d[:], in_=s),
                   writes=[name], dma="dcst", label=f"ld_{name}")
        CT0 = TBF_COLS - 2 * S
        ST0 = TBF_COLS - S
        ctab = TB.take("ctab", CT0, CT0 + S)
        stab = TB.take("stab", ST0, ST0 + S)
        sch.op("sp", lambda e: e.dma_start(out=ctab, in_=ctab_d),
               writes=["ctab"], dma="dcst", label="ld_ctab")
        sch.op("sp", lambda e: e.dma_start(out=stab, in_=stab_d),
               writes=["stab"], dma="dcst", label="ld_stab")
        sch.merge_writers(["ident", "maskut", "onesd", "ones1", "emat", "brs",
                           "ctab", "stab"])
        sch.op("dve", lambda e: e.memset(kbias[:, 0:1], 1e-5),
               writes=["kbias"], label="kb0")
        sch.op("dve", lambda e: e.memset(kbias[:, 1:2], EXP_SHIFT),
               reads=["kbias"], writes=["kbias"], label="kb1")
        sch.op("dve", lambda e: e.memset(kbias[:, 2:3], 0.0),
               reads=["kbias"], writes=["kbias"], label="kb2")

        for _s in range(5):
            TF.take(f"xsl{_s}", _s * 1024, (_s + 1) * 1024)
        xslot = [tf32[:, s * 1024:(s + 1) * 1024] for s in range(5)]
        xnbf = TB.take(["xnbf0", "xnbf1", "xnbf2", "xnbf3"], 0, D)
        st = [st0, st1]

        def emit_x_load(t, k):
            if t >= NT:
                return
            s = (4 * t + k) % 5
            eng = "sp" if k < 2 else "act"
            sch.op(eng, lambda e, t=t, k=k, s=s: e.dma_start(
                out=xslot[s],
                in_=x_d[t * 128:(t + 1) * 128, k * 1024:(k + 1) * 1024]),
                writes=[f"xsl{s}"], dma=f"dx{k}", label=f"ldx{t}.{k}")

        def emit_p1_tile(tb):
            stt = st[tb % 2]
            K = lambda c: f"st{tb % 2}.{c}"
            emit_x_load(tb + 1, 0)
            slots = [(4 * tb + k) % 5 for k in range(4)]
            xq = [xslot[s] for s in slots]
            for k in range(4):
                for hh in range(2):
                    col = (2 * k + hh) * 6
                    sch.op("dve", lambda e, k=k, hh=hh, col=col, stt=stt, xq=xq:
                           e.bn_stats(stt[:, col:col + 6],
                                      xq[k][:, hh * 512:(hh + 1) * 512]),
                           reads=[f"xsl{slots[k]}"], writes=[K(f"bn{2 * k + hh}")],
                           label=f"bn{tb}.{2 * k + hh}")
            sch.op("dve", lambda e, stt=stt: e.bn_aggr(stt[:, 48:50], stt[:, 0:48]),
                   reads=[K(f"bn{i}") for i in range(8)], writes=[K("mv")],
                   label=f"aggr{tb}")
            sch.op("act", lambda e, stt=stt: e.activation(
                out=stt[:, 50:51], in_=stt[:, 49:50], func=AF.Ln, bias=kbias[:, 0:1],
                scale=1.0),
                reads=[K("mv"), "kbias"], writes=[K("lnv")], label=f"ln{tb}")
            sch.op("act", lambda e, stt=stt: e.activation(
                out=stt[:, 51:52], in_=stt[:, 50:51], func=AF.Exp, bias=kbias[:, 2:3],
                scale=-0.5),
                reads=[K("lnv"), "kbias"], writes=[K("rstd")], label=f"rstd{tb}")
            sch.op("dve", lambda e, stt=stt: e.tensor_scalar(
                out=stt[:, 52:53], in0=stt[:, 48:49], scalar1=stt[:, 51:52],
                scalar2=-1.0, op0=ALU.mult, op1=ALU.mult),
                reads=[K("mv"), K("rstd")], writes=[K("negmb")], label=f"negmb{tb}")
            for k in range(4):
                if k < 2:
                    sch.op("act", lambda e, k=k, stt=stt, xq=xq: e.activation(
                        out=xnbf[:, k * 1024:(k + 1) * 1024], in_=xq[k],
                        func=AF.Identity, bias=stt[:, 52:53], scale=stt[:, 51:52]),
                        reads=[f"xsl{slots[k]}", K("rstd"), K("negmb")],
                        writes=[f"xnbf{k}"], label=f"xn{tb}.{k}")
                else:
                    sch.op("dve", lambda e, k=k, stt=stt, xq=xq: e.tensor_scalar(
                        out=xnbf[:, k * 1024:(k + 1) * 1024], in0=xq[k],
                        scalar1=stt[:, 51:52], scalar2=stt[:, 52:53],
                        op0=ALU.mult, op1=ALU.add),
                        reads=[f"xsl{slots[k]}", K("rstd"), K("negmb")],
                        writes=[f"xnbf{k}"], label=f"xn{tb}.{k}")
                emit_x_load(tb + 1, k + 1) if k < 3 else None
            for g in range(NG):
                bank = g % 2
                pdst = psB[:, bank * 512:bank * 512 + 512]
                for u in range(4):
                    c = g * 4 + u
                    qq = (c * 128) // (D // 4)
                    sch.op("pe", lambda e, c=c, u=u, pdst=pdst: e.matmul(
                        pdst[:, u * 128:(u + 1) * 128],
                        xnbf[:, c * 128:(c + 1) * 128], ident[:],
                        start=True, stop=True),
                        reads=[f"xnbf{qq}", "ident"], writes=[f"psB{bank}"],
                        label=f"tr{tb}.{c}")
                def ev(e, g=g, tb=tb, pdst=pdst):
                    dst = xnt[:].rearrange("p (c s) -> p c s", s=S)[
                        :, g * 4:(g + 1) * 4, tb * 128:(tb + 1) * 128]
                    src = pdst.rearrange("p (c s) -> p c s", s=128)
                    return e.tensor_copy(dst, src)
                eng_ev = "dve" if g % 2 == 0 else "act"
                def eva(e, g=g, tb=tb, pdst=pdst):
                    dst = xnt[:].rearrange("p (c s) -> p c s", s=S)[
                        :, g * 4:(g + 1) * 4, tb * 128:(tb + 1) * 128]
                    src = pdst.rearrange("p (c s) -> p c s", s=128)
                    return e.copy(out=dst, in_=src)
                sch.op(eng_ev, ev if eng_ev == "dve" else eva, reads=[f"psB{bank}"],
                       writes=[f"xnt{g * 4 + u}.g{tb // 4}" for u in range(4)],
                       label=f"evtr{tb}.{g}")

        kd = TB.take("kd", 2 * S, 3 * S)
        ks = TB.take("ks", 3 * S, 4 * S)
        qd = tbf[:, 0:S]
        qs = tbf[:, S:2 * S]
        vd = tbf[:, 4 * S:5 * S]
        vout = [tbf[:, 5 * S:5 * S + 512], tbf[:, 5 * S + 512:5 * S + 1024]]

        jobs = [(m, h) for h in range(KC) for m in ("q", "k", "v")]
        wsrc = {"q": wq_d, "k": wk_d, "v": wv_d}
        NGRP = max(1, KC // 8)
        PER = KC // NGRP

        def emit_panel_load(j, src_by_job, label):
            if j >= len(jobs if src_by_job is None else src_by_job):
                return
            if src_by_job is None:
                m, h = jobs[j]
                src_t, col = wsrc[m], jobs[j][1]
            else:
                src_t, col = src_by_job[j]
            buf = j % 2
            for g in range(NGRP):
                def ld(e, src_t=src_t, col=col, buf=buf, g=g):
                    src = src_t.rearrange("(c p) n -> p c n", p=128)[
                        :, g * PER:(g + 1) * PER, col * 128:(col + 1) * 128]
                    dst = wpan[buf][:].rearrange("p (c j) -> p c j", j=128)[
                        :, g * PER:(g + 1) * PER, :]
                    return e.dma_start(out=dst, in_=src)
                sch.op("pool", ld, writes=[f"wpan{buf}.g{g}"], dma=f"dw{buf}",
                       label=f"{label}{j}.{g}")
            sch.merge_writers([f"wpan{buf}.g{g}" for g in range(NGRP)])

        def emit_qk_post(j, m, h, do_evac=True):
            ps, psn = (psA, "psA") if m == "q" else (psB, "psB")
            dtile, dname = (qd, "qd") if m == "q" else (kd, "kd")
            stile, sname = (qs, "qs") if m == "q" else (ks, "ks")
            if do_evac:
                sch.op("act", lambda e, ps=ps, dtile=dtile: e.copy(
                    out=dtile, in_=ps[:, 0:S]),
                    reads=[f"{psn}{b}" for b in range(NB)], writes=[dname],
                    label=f"ev{j}")
            sch.op("dve", lambda e, dtile=dtile, stile=stile: e.stream_shuffle(
                stile, dtile, shuffle_mask),
                reads=[dname], writes=[sname], label=f"shuf{j}")
            sch.op("dve", lambda e, stile=stile: e.tensor_tensor(
                out=stile, in0=stile, in1=stab, op=ALU.mult),
                reads=[sname, "stab"], writes=[sname], label=f"rms{j}")
            sch.op("dve", lambda e, dtile=dtile: e.tensor_tensor(
                out=dtile, in0=dtile, in1=ctab, op=ALU.mult),
                reads=[dname, "ctab"], writes=[dname], label=f"rmc{j}")
            sch.op("dve", lambda e, dtile=dtile, stile=stile: e.tensor_tensor(
                out=dtile, in0=dtile, in1=stile, op=ALU.add),
                reads=[dname, sname], writes=[dname], label=f"rot{j}")
            dst_d = qts_d if m == "q" else kts_d
            dk = f"{'qts' if m == 'q' else 'kts'}{h}"
            sch.op("sp", lambda e, h=h, dtile=dtile, dst_d=dst_d: e.dma_start(
                out=dst_d[h * 128:(h + 1) * 128, :], in_=dtile),
                reads=[dname], writes=[dk],
                dma=("dsq" if m == "q" else "dsk"), label=f"st{j}")

        emit_panel_load(0, None, "wld")
        emit_panel_load(1, None, "wld")
        for _k in range(4):
            emit_x_load(0, _k)

        for g in range(NB):
            for u in range(4):
                emit_p1_tile(g * 4 + u)
            for j in (0, 1):
                buf = j % 2
                ps, psn = (psA, "psA") if j == 0 else (psB, "psB")
                for c in range(KC):
                    sch.op("pe", lambda e, c=c, g=g, buf=buf, ps=ps: e.matmul(
                        ps[:, g * 512:(g + 1) * 512],
                        wpan[buf][:, c * 128:(c + 1) * 128],
                        xnt_sl(c, g * 512, (g + 1) * 512),
                        start=(c == 0), stop=(c == KC - 1)),
                        reads=[f"wpan{buf}.g{c // PER}", f"xnt{c}.g{g}"],
                        writes=[f"{psn}{g}"], label=f"prj{j}.{c}.{g}")
                if j == 1:
                    sch.op("act", lambda e, g=g: e.copy(
                        out=kd[:, g * 512:(g + 1) * 512],
                        in_=psB[:, g * 512:(g + 1) * 512]),
                        reads=[f"psB{g}"], writes=["kd"], label=f"kev{g}")
        TB.take("qd", 0, S)
        TB.take("qs", S, 2 * S)
        TB.take("vd", 4 * S, 5 * S)
        TB.take("vout0", 5 * S, 5 * S + 512)
        TB.take("vout1", 5 * S + 512, 5 * S + 1024)
        emit_panel_load(2, None, "wld")
        emit_panel_load(3, None, "wld")
        emit_qk_post(0, "q", 0, do_evac=True)
        emit_qk_post(1, "k", 0, do_evac=False)

        for j, (m, h) in enumerate(jobs):
            if j < 2:
                continue
            buf = j % 2
            ps, psn = (psA, "psA") if m != "k" else (psB, "psB")
            for c in range(KC):
                for b in range(NB):
                    sch.op("pe", lambda e, c=c, b=b, buf=buf, ps=ps: e.matmul(
                        ps[:, b * 512:(b + 1) * 512],
                        wpan[buf][:, c * 128:(c + 1) * 128],
                        xnt_sl(c, b * 512, (b + 1) * 512),
                        start=(c == 0), stop=(c == KC - 1)),
                        reads=[f"wpan{buf}.g{c // PER}", f"xnt{c}.g{b}"],
                        writes=[f"{psn}{b}"], label=f"prj{j}.{c}.{b}")
            emit_panel_load(j + 2, None, "wld")
            if m in ("q", "k"):
                emit_qk_post(j, m, h, do_evac=True)
            else:
                sch.op("act", lambda e, ps=ps: e.copy(out=vd, in_=ps[:, 0:S]),
                       reads=[f"{psn}{b}" for b in range(NB)], writes=["vd"],
                       label=f"ev{j}")
                for g in range(NT // 4):
                    vb = g % 2
                    pdst = psB[:, vb * 512:vb * 512 + 512]
                    for u in range(4):
                        t = g * 4 + u
                        sch.op("pe", lambda e, t=t, u=u, pdst=pdst: e.matmul(
                            pdst[:, u * 128:(u + 1) * 128],
                            vd[:, t * 128:(t + 1) * 128], ident[:],
                            start=True, stop=True),
                            reads=["vd", "ident"], writes=[f"psB{vb}"],
                            label=f"vtr{j}.{t}")
                    sch.op("dve", lambda e, vb=vb, pdst=pdst: e.tensor_copy(
                        vout[vb], pdst),
                        reads=[f"psB{vb}"], writes=[f"vout{vb}"], label=f"vev{j}.{g}")
                    def vst(e, h=h, g=g, vb=vb):
                        dst = vs_d[h, :, g * 512:(g + 1) * 512]
                        return e.dma_start(out=dst, in_=vout[vb])
                    sch.op("pool", vst, reads=[f"vout{vb}"], writes=[f"vs{h}.g{g}"],
                           dma=f"dsv{vb}", label=f"vst{j}.{g}")

        abf = TB.take("abf", 0, D)
        atn = TB.take("atn", D, 2 * D)
        lowbf = TB.take("lowbf", 2 * D, 2 * D + S)
        sch.op("pool", lambda e: e.dma_start(out=abf, in_=ap_d),
               writes=["abf"], dma="dcp2", label="ld_abf")
        sch.op("pool", lambda e: e.dma_start(
            out=wrtb[:].rearrange("p (c j) -> p c j", j=8),
            in_=wrt_d.rearrange("(c p) j -> p c j", p=128)),
            writes=["wrtb"], dma="dcp2", label="ld_wrtb")
        sch.merge_writers(["abf", "wrtb"])
        for g in range(NG):
            bank = g % 2
            pdst = psB[:, bank * 512:bank * 512 + 512]
            for u in range(4):
                c = g * 4 + u
                sch.op("pe", lambda e, c=c, u=u, pdst=pdst: e.matmul(
                    pdst[:, u * 128:(u + 1) * 128], abf[:, c * 128:(c + 1) * 128],
                    ident[:], start=True, stop=True),
                    reads=["abf", "ident"], writes=[f"psB{bank}"], label=f"atr{c}")
            sch.op("act", lambda e, g=g, pdst=pdst: e.copy(
                out=atn[:, g * 512:(g + 1) * 512], in_=pdst),
                reads=[f"psB{bank}"], writes=["atn"], label=f"aev{g}")
        def emit_low(c):
            for b in range(NB):
                sch.op("pe", lambda e, c=c, b=b: e.matmul(
                    psA[:, b * 512:(b + 1) * 512], atn[:, c * 128:(c + 1) * 128],
                    xnt_sl(c, b * 512, (b + 1) * 512),
                    start=(c == 0), stop=(c == KC - 1)),
                    reads=["atn", f"xnt{c}.g{b}"], writes=[f"psA{b}"],
                    label=f"low{c}.{b}")

        def emit_str(tb):
            sb_ = (tb % 2) * 2 + 1
            sofs = sb_ * 512
            sch.op("pe", lambda e, tb=tb, sofs=sofs: e.matmul(
                psB[0:8, sofs:sofs + 128], snb[:, (tb % 2) * 8:(tb % 2) * 8 + 8],
                ident[:], start=True, stop=True),
                reads=[f"snb{tb % 2}", "ident"], writes=[f"psB{sb_}"], label=f"str{tb}")
            sch.op("act", lambda e, tb=tb, sofs=sofs: e.copy(
                out=scT[:, tb * 128:(tb + 1) * 128], in_=psB[0:8, sofs:sofs + 128]),
                reads=[f"psB{sb_}"], writes=["scT"], label=f"sev{tb}")

        for tb in range(NT):
            lb = (tb % 2) * 2
            lofs = lb * 512
            sch.op("pe", lambda e, lofs=lofs: e.matmul(
                psB[:, lofs:lofs + 8], ones1[:], brs[:], start=True, stop=False),
                reads=["ones1", "brs"], writes=[f"psB{lb}"], label=f"rbias{tb}")
            for c in range(KC):
                sch.op("pe", lambda e, c=c, tb=tb, lofs=lofs: e.matmul(
                    psB[:, lofs:lofs + 8], xnt_sl(c, tb * 128, (tb + 1) * 128),
                    wrtb[:, c * 8:(c + 1) * 8],
                    start=False, stop=(c == KC - 1)),
                    reads=[f"xnt{c}.g{tb // 4}", "wrtb"], writes=[f"psB{lb}"],
                    label=f"rlog{tb}.{c}")
            if tb >= 1:
                emit_str(tb - 1)
            emit_low(2 * tb)
            emit_low(2 * tb + 1)
            rb = (tb % 2) * 8
            sch.op("act", lambda e, lofs=lofs, rb=rb: e.copy(
                out=rsc[:, rb:rb + 8], in_=psB[:, lofs:lofs + 8]),
                reads=[f"psB{lb}"], writes=[f"rsc{tb % 2}"], label=f"rev{tb}")
            sch.op("dve", lambda e, rb=rb: e.reduce_max(
                rsc[:, rb + 16:rb + 17], rsc[:, rb:rb + 8], axis=AX.X),
                reads=[f"rsc{tb % 2}"], writes=[f"rsc{tb % 2}"], label=f"rmax{tb}")
            sch.op("dve", lambda e, rb=rb: e.tensor_scalar(
                out=rsc[:, rb + 17:rb + 18], in0=rsc[:, rb + 16:rb + 17],
                scalar1=-1.0, scalar2=None, op0=ALU.mult),
                reads=[f"rsc{tb % 2}"], writes=[f"rsc{tb % 2}"], label=f"rneg{tb}")
            sch.op("act", lambda e, rb=rb: e.activation(
                out=rsc[:, rb + 32:rb + 40], in_=rsc[:, rb:rb + 8], func=AF.Exp,
                bias=rsc[:, rb + 17:rb + 18], scale=1.0,
                accum_out=rsc[:, rb + 18:rb + 19]),
                reads=[f"rsc{tb % 2}"], writes=[f"rsc{tb % 2}"], label=f"rexp{tb}")
            sch.op("dve", lambda e, rb=rb: e.reciprocal(
                out=rsc[:, rb + 19:rb + 20], in_=rsc[:, rb + 18:rb + 19]),
                reads=[f"rsc{tb % 2}"], writes=[f"rsc{tb % 2}"], label=f"rrec{tb}")
            sch.op("dve", lambda e, rb=rb, tb=tb: e.tensor_scalar(
                out=snb[:, (tb % 2) * 8:(tb % 2) * 8 + 8], in0=rsc[:, rb + 32:rb + 40],
                scalar1=rsc[:, rb + 19:rb + 20], scalar2=16.0,
                op0=ALU.mult, op1=ALU.mult),
                reads=[f"rsc{tb % 2}"], writes=[f"snb{tb % 2}"], label=f"rsc{tb}")
        emit_str(NT - 1)
        for b in range(NB):
            sch.op("act", lambda e, b=b: e.copy(
                out=lowbf[:, b * 512:(b + 1) * 512], in_=psA[:, b * 512:(b + 1) * 512]),
                reads=[f"psA{b}"], writes=["lowbf"], label=f"lowev{b}")
        for b in range(NB):
            sch.op("pe", lambda e, b=b: e.matmul(
                psA[:, b * 512:(b + 1) * 512], emat[:], scT[:, b * 512:(b + 1) * 512],
                start=True, stop=True),
                reads=["emat", "scT"], writes=[f"psA{b}"], label=f"bc{b}")
            sch.op("dve", lambda e, b=b: e.tensor_tensor(
                out=lst[:, b * 512:(b + 1) * 512], in0=lowbf[:, b * 512:(b + 1) * 512],
                in1=psA[:, b * 512:(b + 1) * 512], op=ALU.mult),
                reads=["lowbf", f"psA{b}"], writes=["lst"], label=f"lst{b}")

        qb = [TB.take("qb0", 0, S), TB.take("qb1", S, 2 * S)]
        kb = [TB.take("kb0", 2 * S, 3 * S), TB.take("kb1", 3 * S, 4 * S)]
        TF.take("vbuf0", 2048, 3072)
        TF.take("vbuf1", 1024, 2048)
        vbuf = [tf32[:, 2048:3072].bitcast(BF16),
                tf32[:, 1024:2048].bitcast(BF16)]
        PRB0 = 4 * S
        prb = [TB.take(f"prb{c}", PRB0 + c * 512, PRB0 + (c + 1) * 512)
               for c in range(NT)]
        TF.take(["rcpS0", "rcpS1"], 0, 1024)
        rcpS = [tf32[:, 0:512], tf32[:, 512:1024]]

        def emit_qkv_load(h):
            if h >= KC:
                return
            par = h % 2
            sch.op("sp", lambda e, h=h, par=par: e.dma_start(
                out=qb[par], in_=qts_d[h * 128:(h + 1) * 128, :]),
                reads=[f"qts{h}"], writes=[f"qb{par}"], dma=f"dq{par}", label=f"qld{h}")
            sch.op("sp", lambda e, h=h, par=par: e.dma_start(
                out=kb[par], in_=kts_d[h * 128:(h + 1) * 128, :]),
                reads=[f"kts{h}"], writes=[f"kb{par}"], dma=f"dq{par}", label=f"kld{h}")
            sch.merge_writers([f"qb{par}", f"kb{par}"])

        def emit_v_load(h):
            if h >= KC:
                return
            vp = h % 2
            sch.op("sp", lambda e, h=h, vp=vp: e.dma_start(
                out=vbuf[vp], in_=vs_d[h]),
                reads=[f"vs{h}.g{g}" for g in range(NT // 4)],
                writes=[f"vbuf{vp}"], dma="dv0", label=f"vld{h}")

        emit_qkv_load(0)
        emit_qkv_load(1)
        emit_v_load(0)
        emit_v_load(1)

        pend = []
        for h in range(KC):
            par = h % 2
            vp = h % 2
            for p in range(NB):
                sch.alias(f"attnT{h}.p{p}", [f"xnt{h}.g{p}"])
            for p in range(NB):
                pg = p % 2
                ps, psn = (psA, "psA") if pg == 0 else (psB, "psB")
                qlo = p * 512
                nchunks = 4 * (p + 1)

                def emit_sc(c, p=p, qlo=qlo, ps=ps, psn=psn):
                    sb = c % 2
                    qoff = max(0, c * 128 - qlo)
                    sch.op("pe", lambda e, c=c, sb=sb, qoff=qoff, qlo=qlo,
                           par=par, ps=ps: e.matmul(
                        ps[:, sb * 512 + qoff: (sb + 1) * 512],
                        kb[par][:, c * 128:(c + 1) * 128],
                        qb[par][:, qlo + qoff: qlo + 512],
                        start=True, stop=True),
                        reads=[f"kb{par}", f"qb{par}"], writes=[f"{psn}{sb}"],
                        label=f"sc{h}.{p}.{c}")
                    sch.op("act", lambda e, c=c, sb=sb, qoff=qoff, ps=ps: e.activation(
                        out=prb[c][:, qoff:512],
                        in_=ps[:, sb * 512 + qoff: (sb + 1) * 512],
                        func=AF.Exp, bias=kbias[:, 1:2], scale=SC),
                        reads=[f"{psn}{sb}", "kbias"], writes=[f"prb{c}"],
                        label=f"ex{h}.{p}.{c}")
                    if c * 128 >= qlo:
                        sch.op("dve", lambda e, c=c, qoff=qoff: e.tensor_tensor(
                            out=prb[c][:, qoff:qoff + 128],
                            in0=prb[c][:, qoff:qoff + 128],
                            in1=maskut[:], op=ALU.mult),
                            reads=[f"prb{c}", "maskut"], writes=[f"prb{c}"],
                            label=f"mk{h}.{p}.{c}")

                def emit_rspv(c, first, last, p=p, qlo=qlo, ps=ps, psn=psn):
                    qoff = max(0, c * 128 - qlo)
                    sch.op("pe", lambda e, c=c, qoff=qoff, ps=ps,
                           first=first, last=last: e.matmul(
                        ps[:, 1536 + qoff:2048],
                        onesd[:], prb[c][:, qoff:512],
                        start=first, stop=last),
                        reads=["onesd", f"prb{c}"], writes=[f"{psn}3"],
                        label=f"rs{h}.{p}.{c}")
                    sch.op("pe", lambda e, c=c, qoff=qoff, ps=ps, vp=vp,
                           first=first, last=last: e.matmul(
                        ps[:, 1024 + qoff:1536],
                        vbuf[vp][:, c * 128:(c + 1) * 128],
                        prb[c][:, qoff:512],
                        start=first, stop=last),
                        reads=[f"vbuf{vp}", f"prb{c}"], writes=[f"{psn}2"],
                        label=f"pv{h}.{p}.{c}")

                order = ([0] + list(range(max(1, 4 * p), nchunks))
                         + list(range(1, 4 * p)))
                assert len(order) == nchunks and sorted(order) == list(range(nchunks))
                flush_pos = min(4, nchunks - 1)
                for i, c in enumerate(order):
                    emit_sc(c)
                    if i == flush_pos and pend:
                        for f in pend:
                            f()
                        pend = []
                    if i >= 2:
                        emit_rspv(order[i - 2], i - 2 == 0, i - 2 == nchunks - 1)
                emit_rspv(order[nchunks - 2], False, False)
                emit_rspv(order[nchunks - 1], False, True)

                def mk_tail(h=h, p=p, ps=ps, psn=psn, pg=pg, qlo=qlo):
                    def frec():
                        sch.op("dve", lambda e: e.reciprocal(
                            out=rcpS[pg], in_=ps[:, 1536:2048]),
                            reads=[f"{psn}3"], writes=[f"rcpS{pg}"],
                            label=f"rec{h}.{p}")

                    def faev():
                        sch.op("dve", lambda e: e.tensor_tensor(
                            out=xnt_sl(h, qlo, qlo + 512), in0=ps[:, 1024:1536],
                            in1=rcpS[pg], op=ALU.mult),
                            reads=[f"{psn}2", f"rcpS{pg}"],
                            writes=[f"attnT{h}.p{p}"], label=f"aev{h}.{p}")
                    return [frec, faev]
                pend = mk_tail()
            emit_qkv_load(h + 2)
            emit_v_load(h + 2)
        for f in pend:
            f()
        pend = []

        sT_t = [TB.take("sT0", 0, S), TB.take("sT1", S + D, 2 * S + D)]
        bst = TB.take("bst", S, S + D)
        xcol = TF.take("xcol", 0, S)
        ocol = TF.take("ocol", S, 2 * S)
        sch.op("pool", lambda e: e.dma_start(out=bst, in_=bp_d),
               writes=["bst"], dma="dbst", label="ld_bst")

        def emit_wo_load(ob):
            if ob >= KC:
                return
            buf = ob % 2
            for g in range(NGRP):
                def ld(e, ob=ob, buf=buf, g=g):
                    src = wo_d.rearrange("(c p) n -> p c n", p=128)[
                        :, g * PER:(g + 1) * PER, ob * 128:(ob + 1) * 128]
                    dst = wpan[buf][:].rearrange("p (c j) -> p c j", j=128)[
                        :, g * PER:(g + 1) * PER, :]
                    return e.dma_start(out=dst, in_=src)
                sch.op("pool", ld, writes=[f"wpan{buf}.g{g}"], dma=f"dw{buf}",
                       label=f"wold{ob}.{g}")
            sch.merge_writers([f"wpan{buf}.g{g}" for g in range(NGRP)])

        def emit_xcl(ob):
            def xld(e, ob=ob):
                src = x_d.rearrange("(a p) n -> p a n", p=128)[
                    :, :, ob * 128:(ob + 1) * 128]
                return e.dma_start(out=xcol.rearrange("p (a n) -> p a n", n=128),
                                   in_=src)
            sch.op("sp", xld, reads=[], writes=["xcol"], dma="dxc", label=f"xcl{ob}")

        def emit_tail(ob):
            sT = sT_t[ob % 2]
            sn = f"sT{ob % 2}"
            for g in range(NT // 4):
                vb = g % 2
                pdst = psB[:, vb * 512:vb * 512 + 512]
                for u in range(4):
                    t = g * 4 + u
                    sch.op("pe", lambda e, t=t, u=u, pdst=pdst, sT=sT: e.matmul(
                        pdst[:, u * 128:(u + 1) * 128], sT[:, t * 128:(t + 1) * 128],
                        ident[:], start=True, stop=True),
                        reads=[sn, "ident"], writes=[f"psB{vb}"],
                        label=f"ftr{ob}.{t}")
                sch.op("dve", lambda e, g=g, pdst=pdst: e.tensor_tensor(
                    out=ocol[:, g * 512:(g + 1) * 512], in0=pdst,
                    in1=xcol[:, g * 512:(g + 1) * 512], op=ALU.add),
                    reads=[f"psB{vb}", "xcol"], writes=["ocol"], label=f"fadd{ob}.{g}")
            def ost(e, ob=ob):
                dst = out_d.rearrange("(a p) n -> p a n", p=128)[
                    :, :, ob * 128:(ob + 1) * 128]
                return e.dma_start(out=dst,
                                   in_=ocol.rearrange("p (a n) -> p a n", n=128))
            sch.op("sp", ost, reads=["ocol"], writes=[f"outc{ob}"], dma="dou",
                   label=f"ost{ob}")

        emit_wo_load(0)
        emit_wo_load(1)
        emit_xcl(0)

        for ob in range(KC):
            buf = ob % 2
            for c in range(KC):
                for b in range(NB):
                    sch.op("pe", lambda e, c=c, b=b, buf=buf: e.matmul(
                        psA[:, b * 512:(b + 1) * 512],
                        wpan[buf][:, c * 128:(c + 1) * 128],
                        xnt_sl(c, b * 512, (b + 1) * 512),
                        start=(c == 0), stop=False),
                        reads=[f"wpan{buf}.g{c // PER}", f"attnT{c}.p{b}"],
                        writes=[f"psA{b}"], label=f"op{ob}.{c}.{b}")
            emit_wo_load(ob + 2)
            for b in range(NB):
                sch.op("pe", lambda e, b=b, ob=ob: e.matmul(
                    psA[:, b * 512:(b + 1) * 512], bst[:, ob * 128:(ob + 1) * 128],
                    lst[:, b * 512:(b + 1) * 512], start=False, stop=True),
                    reads=["bst", "lst"], writes=[f"psA{b}"], label=f"fu{ob}.{b}")
            if ob > 0:
                emit_tail(ob - 1)
                emit_xcl(ob)
            sch.op("act", lambda e, ob=ob: e.copy(out=sT_t[ob % 2], in_=psA[:, 0:S]),
                   reads=[f"psA{b}" for b in range(NB)], writes=[f"sT{ob % 2}"],
                   label=f"sev{ob}")
        emit_tail(KC - 1)

        finals = dict(sch.cnt)

        @block.tensor
        def _(eng):
            sch.emit("pe", eng, sems)

        @block.scalar
        def _(eng):
            sch.emit("act", eng, sems)

        @block.vector
        def _(eng):
            sch.emit("dve", eng, sems)

        @block.gpsimd
        def _(eng):
            sch.emit("pool", eng, sems)

        @block.sync
        def _(eng):
            sch.emit("sp", eng, sems)
            for s in DMA_SEMS + ("pe", "act", "dve", "pool"):
                if s in finals:
                    eng.wait_ge(sems[s], finals[s])

    return nc


def make_consts(S, D):
    HD = 128
    c = {}
    c["ident_hbm"] = np.eye(128, dtype=ml_dtypes.bfloat16)
    i, j = np.mgrid[0:128, 0:128]
    c["maskut_hbm"] = (j >= i).astype(ml_dtypes.bfloat16)
    c["onesd_hbm"] = np.ones((128, 128), dtype=ml_dtypes.bfloat16)
    c["ones1_hbm"] = np.ones((1, 128), dtype=ml_dtypes.bfloat16)
    li = np.arange(128)
    e = np.zeros((8, 128), dtype=np.float32)
    e[li % 8, li] = 1.0
    c["emat_hbm"] = e.astype(ml_dtypes.bfloat16)
    inv = 1.0 / 10000.0 ** (np.arange(0, HD, 2)[: HD // 2].astype(np.float32) / HD)
    t = np.arange(S)
    f = np.outer(t, inv).astype(np.float32)
    cos = np.cos(f)
    sin = np.sin(f)
    p = np.arange(128)
    ctab = cos[:, p // 2].T
    sgn = np.where(p % 2 == 0, -1.0, 1.0).astype(np.float32)
    stab = (sin[:, p // 2] * sgn[None, :]).T
    c["ctab_hbm"] = np.ascontiguousarray(ctab).astype(ml_dtypes.bfloat16)
    c["stab_hbm"] = np.ascontiguousarray(stab).astype(ml_dtypes.bfloat16)
    return c


def make_in_maps(x, wq, wk, wv, wo, A, B, Wr, br, S, D):
    shared = {
        "wq": np.ascontiguousarray(wq, dtype=np.float32),
        "wk": np.ascontiguousarray(wk, dtype=np.float32),
        "wv": np.ascontiguousarray(wv, dtype=np.float32),
        "wo": np.ascontiguousarray(wo, dtype=np.float32),
        "ap": np.ascontiguousarray(
            np.asarray(A, dtype=np.float32).transpose(1, 0, 2).reshape(128, D)),
        "bp": np.ascontiguousarray(
            np.asarray(B, dtype=np.float32).transpose(2, 0, 1).reshape(128, D)),
        "wrt": np.ascontiguousarray(np.asarray(Wr, dtype=np.float32).T),
        "brs_hbm": np.asarray(br, dtype=np.float32).reshape(1, 8).astype(ml_dtypes.bfloat16),
    }
    shared.update(make_consts(S, D))
    maps = []
    for b in range(x.shape[0]):
        m = dict(shared)
        m["x"] = np.ascontiguousarray(x[b], dtype=np.float32)
        maps.append(m)
    return maps


def kernel(L0_attn_x, L0_attn_wq, L0_attn_wk, L0_attn_wv, L0_attn_wo,
           L0_ffn_A, L0_ffn_B, L0_ffn_Wr, L0_ffn_br):
    from concourse.bass_utils import run_bass_kernel_spmd
    S, D = 2048, 4096
    x = np.asarray(L0_attn_x, dtype=np.float32)
    nc = build_program(S, D)
    maps = make_in_maps(x, L0_attn_wq, L0_attn_wk, L0_attn_wv, L0_attn_wo,
                        L0_ffn_A, L0_ffn_B, L0_ffn_Wr, L0_ffn_br, S, D)
    res = run_bass_kernel_spmd(nc, maps, core_ids=list(range(8)))
    return np.stack([r["out"] for r in res.results], axis=0)


# version 13 (final)
# speedup vs baseline: 1.0170x; 1.0170x over previous
import math
import numpy as np
import ml_dtypes
import concourse.bass as bass
import concourse.mybir as mybir

F32 = mybir.dt.float32
BF16 = mybir.dt.bfloat16
AF = mybir.ActivationFunctionType
ALU = mybir.AluOpType
AX = mybir.AxisListType

ENGS = ("pe", "act", "dve", "pool", "sp")
DMA_SEMS = ("dcst", "dx0", "dx1", "dx2", "dx3", "dw0", "dw1", "dq0", "dq1",
            "dv0", "dsq", "dsk", "dsv0", "dsv1", "dxc", "dou", "dcp2", "dbst")
EXP_SHIFT = -30.0


class Sched:

    def __init__(self):
        self.q = {e: [] for e in ENGS}
        self.cnt = {}
        self.waited = {e: {} for e in ENGS}
        self.w = {}
        self.r = {}

    def _need(self, eng, tok):
        s, v = tok
        if eng == "pe" and s == "pe":
            return False
        return self.waited[eng].get(s, 0) < v

    def op(self, eng, fn, reads=(), writes=(), dma=None, label=""):
        deps = []
        for b in reads:
            t = self.w.get(b)
            if t:
                deps.append(t)
        for b in writes:
            t = self.w.get(b)
            if t:
                deps.append(t)
            for s, v in self.r.get(b, {}).items():
                deps.append((s, v))
        waits = []
        for t in deps:
            if self._need(eng, t):
                waits.append(t)
                self.waited[eng][t[0]] = max(self.waited[eng].get(t[0], 0), t[1])
        sem = dma if dma else eng
        amt = 16 if dma else 1
        self.cnt[sem] = self.cnt.get(sem, 0) + amt
        tok = (sem, self.cnt[sem])
        for b in writes:
            self.w[b] = tok
            self.r[b] = {}
        for b in reads:
            d = self.r.setdefault(b, {})
            if d.get(sem, 0) < tok[1]:
                d[sem] = tok[1]
        self.q[eng].append((waits, fn, sem, amt, label))
        return tok

    def alias(self, new, olds):
        toks = {}
        for o in olds:
            t = self.w.get(o)
            if t and toks.get(t[0], 0) < t[1]:
                toks[t[0]] = t[1]
            for s, v in self.r.get(o, {}).items():
                if toks.get(s, 0) < v:
                    toks[s] = v
        if toks:
            d = self.r.setdefault(new, {})
            for s, v in toks.items():
                if d.get(s, 0) < v:
                    d[s] = v

    def merge_writers(self, bufs):
        best = {}
        for b in bufs:
            t = self.w.get(b)
            if t and best.get(t[0], (0,))[0] < t[1]:
                best[t[0]] = (t[1],)
        assert len(best) <= 1, best
        for b in bufs:
            t = self.w.get(b)
            if t:
                self.w[b] = (t[0], best[t[0]][0])

    def emit(self, eng, proxy, sems):
        for waits, fn, sem, amt, label in self.q[eng]:
            for s, v in waits:
                proxy.wait_ge(sems[s], v)
            inst = fn(proxy)
            inst.then_inc(sems[sem], amt)


class Arena:

    def __init__(self, sched, tensor):
        self.sched = sched
        self.t = tensor
        self.occ = []

    def take(self, bufname, lo, hi):
        names = [bufname] if isinstance(bufname, str) else list(bufname)
        olds = [b for (l, h, bs) in self.occ if l < hi and h > lo for b in bs]
        if olds:
            for n in names:
                self.sched.alias(n, olds)
        newocc = []
        for (l, h, bs) in self.occ:
            if l < hi and h > lo:
                if l < lo:
                    newocc.append((l, lo, bs))
                if h > hi:
                    newocc.append((hi, h, bs))
            else:
                newocc.append((l, h, bs))
        newocc.append((lo, hi, names))
        self.occ = newocc
        return self.t[:, lo:hi]


def build_program(S=2048, D=4096):
    assert S % 512 == 0 and D % 256 == 0
    KC = D // 128
    NT = S // 128
    NB = S // 512
    NG = KC // 4
    assert KC % 4 == 0 and NT % 4 == 0
    SC = 1.0 / math.sqrt(128.0)
    HD2 = D // 2

    TBF_COLS = max(8 * S, 2 * D + S, S + D + 1024)

    nc = bass.Bass("TRN2", target_bir_lowering=False)

    x_d = nc.dram_tensor("x", [S, D], F32, kind="ExternalInput").ap()
    wq_d = nc.dram_tensor("wq", [D, D], F32, kind="ExternalInput").ap()
    wk_d = nc.dram_tensor("wk", [D, D], F32, kind="ExternalInput").ap()
    wv_d = nc.dram_tensor("wv", [D, D], F32, kind="ExternalInput").ap()
    wo_d = nc.dram_tensor("wo", [D, D], F32, kind="ExternalInput").ap()
    ap_d = nc.dram_tensor("ap", [128, D], F32, kind="ExternalInput").ap()
    bp_d = nc.dram_tensor("bp", [128, D], F32, kind="ExternalInput").ap()
    wrt_d = nc.dram_tensor("wrt", [D, 8], F32, kind="ExternalInput").ap()
    brs_d = nc.dram_tensor("brs_hbm", [1, 8], BF16, kind="ExternalInput").ap()
    ident_d = nc.dram_tensor("ident_hbm", [128, 128], BF16, kind="ExternalInput").ap()
    maskut_d = nc.dram_tensor("maskut_hbm", [128, 128], BF16, kind="ExternalInput").ap()
    onesd_d = nc.dram_tensor("onesd_hbm", [128, 128], BF16, kind="ExternalInput").ap()
    ones1_d = nc.dram_tensor("ones1_hbm", [1, 128], BF16, kind="ExternalInput").ap()
    emat_d = nc.dram_tensor("emat_hbm", [8, 128], BF16, kind="ExternalInput").ap()
    ctab_d = nc.dram_tensor("ctab_hbm", [128, S], BF16, kind="ExternalInput").ap()
    stab_d = nc.dram_tensor("stab_hbm", [128, S], BF16, kind="ExternalInput").ap()
    out_d = nc.dram_tensor("out", [S, D], F32, kind="ExternalOutput").ap()
    qts_d = nc.dram_tensor("qts", [D, S], BF16, kind="Internal").ap()
    kts_d = nc.dram_tensor("kts", [D, S], BF16, kind="Internal").ap()
    vs_d = nc.dram_tensor("vs", [KC, 128, S], BF16, kind="Internal").ap()

    sch = Sched()
    shuffle_mask = [i ^ 1 for i in range(32)]

    from contextlib import ExitStack
    with ExitStack() as _es:
        ec = _es.enter_context
        ident = ec(nc.sbuf_tensor("ident", [128, 128], BF16))
        maskut = ec(nc.sbuf_tensor("maskut", [128, 128], BF16))
        onesd = ec(nc.sbuf_tensor("onesd", [128, 128], BF16))
        ones1 = ec(nc.sbuf_tensor("ones1", [1, 128], BF16))
        emat = ec(nc.sbuf_tensor("emat", [8, 128], BF16))
        brs = ec(nc.sbuf_tensor("brs", [1, 8], BF16))
        xnt = ec(nc.sbuf_tensor("xnt", [128, KC * S], BF16))
        lst = ec(nc.sbuf_tensor("lst", [128, S], BF16))
        wpan0 = ec(nc.sbuf_tensor("wpan0", [128, D], BF16))
        wpan1 = ec(nc.sbuf_tensor("wpan1", [128, D], BF16))
        wrtb = ec(nc.sbuf_tensor("wrtb", [128, KC * 8], BF16))
        snb = ec(nc.sbuf_tensor("snb", [128, 16], BF16))
        scT = ec(nc.sbuf_tensor("scT", [8, S], BF16))
        rsc = ec(nc.sbuf_tensor("rsc", [128, 48], F32))
        st0 = ec(nc.sbuf_tensor("st0", [128, 64], F32))
        st1 = ec(nc.sbuf_tensor("st1", [128, 64], F32))
        kbias = ec(nc.sbuf_tensor("kbias", [128, 4], F32))
        tbf = ec(nc.sbuf_tensor("tbf", [128, TBF_COLS], BF16))
        tf32 = ec(nc.sbuf_tensor("tf32", [128, max(2 * S, D) + 1024], F32))
        psA = ec(nc.psum_tensor("psA", [128, 2048], F32))
        psB = ec(nc.psum_tensor("psB", [128, 2048], F32))
        sems = {}
        for s in ("pe", "act", "dve", "pool") + DMA_SEMS:
            sems[s] = ec(nc.semaphore(s))
        block = ec(nc.Block())
        wpan = [wpan0, wpan1]
        TB = Arena(sch, tbf)
        TF = Arena(sch, tf32)

        def xnt_sl(c, lo, hi):
            return xnt[:, c * S + lo: c * S + hi]

        for name, dst, src in [
            ("ident", ident, ident_d), ("maskut", maskut, maskut_d),
            ("onesd", onesd, onesd_d), ("ones1", ones1, ones1_d),
            ("emat", emat, emat_d), ("brs", brs, brs_d),
        ]:
            sch.op("sp", lambda e, d=dst, s=src: e.dma_start(out=d[:], in_=s),
                   writes=[name], dma="dcst", label=f"ld_{name}")
        CT0 = TBF_COLS - 2 * S
        ST0 = TBF_COLS - S
        ctab = TB.take("ctab", CT0, CT0 + S)
        stab = TB.take("stab", ST0, ST0 + S)
        sch.op("sp", lambda e: e.dma_start(out=ctab, in_=ctab_d),
               writes=["ctab"], dma="dcst", label="ld_ctab")
        sch.op("sp", lambda e: e.dma_start(out=stab, in_=stab_d),
               writes=["stab"], dma="dcst", label="ld_stab")
        sch.merge_writers(["ident", "maskut", "onesd", "ones1", "emat", "brs",
                           "ctab", "stab"])
        sch.op("dve", lambda e: e.memset(kbias[:, 0:1], 1e-5),
               writes=["kbias"], label="kb0")
        sch.op("dve", lambda e: e.memset(kbias[:, 1:2], EXP_SHIFT),
               reads=["kbias"], writes=["kbias"], label="kb1")
        sch.op("dve", lambda e: e.memset(kbias[:, 2:3], 0.0),
               reads=["kbias"], writes=["kbias"], label="kb2")

        for _s in range(5):
            TF.take(f"xsl{_s}", _s * 1024, (_s + 1) * 1024)
        xslot = [tf32[:, s * 1024:(s + 1) * 1024] for s in range(5)]
        xnbf = TB.take(["xnbf0", "xnbf1", "xnbf2", "xnbf3"], 0, D)
        st = [st0, st1]

        def emit_x_load(t, k):
            if t >= NT:
                return
            s = (4 * t + k) % 5
            eng = "sp" if k < 2 else "act"
            sch.op(eng, lambda e, t=t, k=k, s=s: e.dma_start(
                out=xslot[s],
                in_=x_d[t * 128:(t + 1) * 128, k * 1024:(k + 1) * 1024]),
                writes=[f"xsl{s}"], dma=f"dx{k}", label=f"ldx{t}.{k}")

        def emit_p1_tile(tb):
            stt = st[tb % 2]
            K = lambda c: f"st{tb % 2}.{c}"
            emit_x_load(tb + 1, 0)
            slots = [(4 * tb + k) % 5 for k in range(4)]
            xq = [xslot[s] for s in slots]
            for k in range(4):
                for hh in range(2):
                    col = (2 * k + hh) * 6
                    sch.op("dve", lambda e, k=k, hh=hh, col=col, stt=stt, xq=xq:
                           e.bn_stats(stt[:, col:col + 6],
                                      xq[k][:, hh * 512:(hh + 1) * 512]),
                           reads=[f"xsl{slots[k]}"], writes=[K(f"bn{2 * k + hh}")],
                           label=f"bn{tb}.{2 * k + hh}")
            sch.op("dve", lambda e, stt=stt: e.bn_aggr(stt[:, 48:50], stt[:, 0:48]),
                   reads=[K(f"bn{i}") for i in range(8)], writes=[K("mv")],
                   label=f"aggr{tb}")
            sch.op("act", lambda e, stt=stt: e.activation(
                out=stt[:, 50:51], in_=stt[:, 49:50], func=AF.Ln, bias=kbias[:, 0:1],
                scale=1.0),
                reads=[K("mv"), "kbias"], writes=[K("lnv")], label=f"ln{tb}")
            sch.op("act", lambda e, stt=stt: e.activation(
                out=stt[:, 51:52], in_=stt[:, 50:51], func=AF.Exp, bias=kbias[:, 2:3],
                scale=-0.5),
                reads=[K("lnv"), "kbias"], writes=[K("rstd")], label=f"rstd{tb}")
            sch.op("dve", lambda e, stt=stt: e.tensor_scalar(
                out=stt[:, 52:53], in0=stt[:, 48:49], scalar1=stt[:, 51:52],
                scalar2=-1.0, op0=ALU.mult, op1=ALU.mult),
                reads=[K("mv"), K("rstd")], writes=[K("negmb")], label=f"negmb{tb}")
            for k in range(4):
                if k < 2:
                    sch.op("act", lambda e, k=k, stt=stt, xq=xq: e.activation(
                        out=xnbf[:, k * 1024:(k + 1) * 1024], in_=xq[k],
                        func=AF.Identity, bias=stt[:, 52:53], scale=stt[:, 51:52]),
                        reads=[f"xsl{slots[k]}", K("rstd"), K("negmb")],
                        writes=[f"xnbf{k}"], label=f"xn{tb}.{k}")
                else:
                    sch.op("dve", lambda e, k=k, stt=stt, xq=xq: e.tensor_scalar(
                        out=xnbf[:, k * 1024:(k + 1) * 1024], in0=xq[k],
                        scalar1=stt[:, 51:52], scalar2=stt[:, 52:53],
                        op0=ALU.mult, op1=ALU.add),
                        reads=[f"xsl{slots[k]}", K("rstd"), K("negmb")],
                        writes=[f"xnbf{k}"], label=f"xn{tb}.{k}")
                emit_x_load(tb + 1, k + 1) if k < 3 else None
            for g in range(NG):
                bank = g % 2
                pdst = psB[:, bank * 512:bank * 512 + 512]
                for u in range(4):
                    c = g * 4 + u
                    qq = (c * 128) // (D // 4)
                    sch.op("pe", lambda e, c=c, u=u, pdst=pdst: e.matmul(
                        pdst[:, u * 128:(u + 1) * 128],
                        xnbf[:, c * 128:(c + 1) * 128], ident[:],
                        start=True, stop=True),
                        reads=[f"xnbf{qq}", "ident"], writes=[f"psB{bank}"],
                        label=f"tr{tb}.{c}")
                def ev(e, g=g, tb=tb, pdst=pdst):
                    dst = xnt[:].rearrange("p (c s) -> p c s", s=S)[
                        :, g * 4:(g + 1) * 4, tb * 128:(tb + 1) * 128]
                    src = pdst.rearrange("p (c s) -> p c s", s=128)
                    return e.tensor_copy(dst, src)
                eng_ev = "dve" if g % 2 == 0 else "act"
                def eva(e, g=g, tb=tb, pdst=pdst):
                    dst = xnt[:].rearrange("p (c s) -> p c s", s=S)[
                        :, g * 4:(g + 1) * 4, tb * 128:(tb + 1) * 128]
                    src = pdst.rearrange("p (c s) -> p c s", s=128)
                    return e.copy(out=dst, in_=src)
                sch.op(eng_ev, ev if eng_ev == "dve" else eva, reads=[f"psB{bank}"],
                       writes=[f"xnt{g * 4 + u}.g{tb // 4}" for u in range(4)],
                       label=f"evtr{tb}.{g}")

        kd = TB.take("kd", 2 * S, 3 * S)
        ks = TB.take("ks", 3 * S, 4 * S)
        qd = tbf[:, 0:S]
        qs = tbf[:, S:2 * S]
        vd = tbf[:, 4 * S:5 * S]
        vout = [tbf[:, 5 * S:5 * S + 512], tbf[:, 5 * S + 512:5 * S + 1024]]

        jobs = [(m, h) for h in range(KC) for m in ("q", "k", "v")]
        wsrc = {"q": wq_d, "k": wk_d, "v": wv_d}
        NGRP = max(1, KC // 8)
        PER = KC // NGRP

        def emit_panel_load(j, src_by_job, label):
            if j >= len(jobs if src_by_job is None else src_by_job):
                return
            if src_by_job is None:
                m, h = jobs[j]
                src_t, col = wsrc[m], jobs[j][1]
            else:
                src_t, col = src_by_job[j]
            buf = j % 2
            for g in range(NGRP):
                def ld(e, src_t=src_t, col=col, buf=buf, g=g):
                    src = src_t.rearrange("(c p) n -> p c n", p=128)[
                        :, g * PER:(g + 1) * PER, col * 128:(col + 1) * 128]
                    dst = wpan[buf][:].rearrange("p (c j) -> p c j", j=128)[
                        :, g * PER:(g + 1) * PER, :]
                    return e.dma_start(out=dst, in_=src)
                sch.op("pool", ld, writes=[f"wpan{buf}.g{g}"], dma=f"dw{buf}",
                       label=f"{label}{j}.{g}")
            sch.merge_writers([f"wpan{buf}.g{g}" for g in range(NGRP)])

        def emit_qk_post(j, m, h, do_evac=True):
            ps, psn = (psA, "psA") if m == "q" else (psB, "psB")
            dtile, dname = (qd, "qd") if m == "q" else (kd, "kd")
            stile, sname = (qs, "qs") if m == "q" else (ks, "ks")
            if do_evac:
                sch.op("act", lambda e, ps=ps, dtile=dtile: e.copy(
                    out=dtile, in_=ps[:, 0:S]),
                    reads=[f"{psn}{b}" for b in range(NB)], writes=[dname],
                    label=f"ev{j}")
            sch.op("dve", lambda e, dtile=dtile, stile=stile: e.stream_shuffle(
                stile, dtile, shuffle_mask),
                reads=[dname], writes=[sname], label=f"shuf{j}")
            sch.op("dve", lambda e, stile=stile: e.tensor_tensor(
                out=stile, in0=stile, in1=stab, op=ALU.mult),
                reads=[sname, "stab"], writes=[sname], label=f"rms{j}")
            sch.op("dve", lambda e, dtile=dtile: e.tensor_tensor(
                out=dtile, in0=dtile, in1=ctab, op=ALU.mult),
                reads=[dname, "ctab"], writes=[dname], label=f"rmc{j}")
            sch.op("dve", lambda e, dtile=dtile, stile=stile: e.tensor_tensor(
                out=dtile, in0=dtile, in1=stile, op=ALU.add),
                reads=[dname, sname], writes=[dname], label=f"rot{j}")
            dst_d = qts_d if m == "q" else kts_d
            dk = f"{'qts' if m == 'q' else 'kts'}{h}"
            sch.op("sp", lambda e, h=h, dtile=dtile, dst_d=dst_d: e.dma_start(
                out=dst_d[h * 128:(h + 1) * 128, :], in_=dtile),
                reads=[dname], writes=[dk],
                dma=("dsq" if m == "q" else "dsk"), label=f"st{j}")

        emit_panel_load(0, None, "wld")
        emit_panel_load(1, None, "wld")
        for _k in range(4):
            emit_x_load(0, _k)

        for g in range(NB):
            for u in range(4):
                emit_p1_tile(g * 4 + u)
            for j in (0, 1):
                buf = j % 2
                ps, psn = (psA, "psA") if j == 0 else (psB, "psB")
                for c in range(KC):
                    sch.op("pe", lambda e, c=c, g=g, buf=buf, ps=ps: e.matmul(
                        ps[:, g * 512:(g + 1) * 512],
                        wpan[buf][:, c * 128:(c + 1) * 128],
                        xnt_sl(c, g * 512, (g + 1) * 512),
                        start=(c == 0), stop=(c == KC - 1)),
                        reads=[f"wpan{buf}.g{c // PER}", f"xnt{c}.g{g}"],
                        writes=[f"{psn}{g}"], label=f"prj{j}.{c}.{g}")
                if j == 1:
                    sch.op("act", lambda e, g=g: e.copy(
                        out=kd[:, g * 512:(g + 1) * 512],
                        in_=psB[:, g * 512:(g + 1) * 512]),
                        reads=[f"psB{g}"], writes=["kd"], label=f"kev{g}")
        TB.take("qd", 0, S)
        TB.take("qs", S, 2 * S)
        TB.take("vd", 4 * S, 5 * S)
        TB.take("vout0", 5 * S, 5 * S + 512)
        TB.take("vout1", 5 * S + 512, 5 * S + 1024)
        emit_panel_load(2, None, "wld")
        emit_panel_load(3, None, "wld")
        emit_qk_post(0, "q", 0, do_evac=True)
        emit_qk_post(1, "k", 0, do_evac=False)

        for j, (m, h) in enumerate(jobs):
            if j < 2:
                continue
            buf = j % 2
            ps, psn = (psA, "psA") if m != "k" else (psB, "psB")
            for c in range(KC):
                for b in range(NB):
                    sch.op("pe", lambda e, c=c, b=b, buf=buf, ps=ps: e.matmul(
                        ps[:, b * 512:(b + 1) * 512],
                        wpan[buf][:, c * 128:(c + 1) * 128],
                        xnt_sl(c, b * 512, (b + 1) * 512),
                        start=(c == 0), stop=(c == KC - 1)),
                        reads=[f"wpan{buf}.g{c // PER}", f"xnt{c}.g{b}"],
                        writes=[f"{psn}{b}"], label=f"prj{j}.{c}.{b}")
            emit_panel_load(j + 2, None, "wld")
            if m in ("q", "k"):
                emit_qk_post(j, m, h, do_evac=True)
            else:
                sch.op("act", lambda e, ps=ps: e.copy(out=vd, in_=ps[:, 0:S]),
                       reads=[f"{psn}{b}" for b in range(NB)], writes=["vd"],
                       label=f"ev{j}")
                for g in range(NT // 4):
                    vb = g % 2
                    pdst = psB[:, vb * 512:vb * 512 + 512]
                    for u in range(4):
                        t = g * 4 + u
                        sch.op("pe", lambda e, t=t, u=u, pdst=pdst: e.matmul(
                            pdst[:, u * 128:(u + 1) * 128],
                            vd[:, t * 128:(t + 1) * 128], ident[:],
                            start=True, stop=True),
                            reads=["vd", "ident"], writes=[f"psB{vb}"],
                            label=f"vtr{j}.{t}")
                    sch.op("dve", lambda e, vb=vb, pdst=pdst: e.tensor_copy(
                        vout[vb], pdst),
                        reads=[f"psB{vb}"], writes=[f"vout{vb}"], label=f"vev{j}.{g}")
                    def vst(e, h=h, g=g, vb=vb):
                        dst = vs_d[h, :, g * 512:(g + 1) * 512]
                        return e.dma_start(out=dst, in_=vout[vb])
                    sch.op("pool", vst, reads=[f"vout{vb}"], writes=[f"vs{h}.g{g}"],
                           dma=f"dsv{vb}", label=f"vst{j}.{g}")

        abf = TB.take("abf", 0, D)
        atn = TB.take("atn", D, 2 * D)
        lowbf = TB.take("lowbf", 2 * D, 2 * D + S)
        sch.op("pool", lambda e: e.dma_start(out=abf, in_=ap_d),
               writes=["abf"], dma="dcp2", label="ld_abf")
        sch.op("pool", lambda e: e.dma_start(
            out=wrtb[:].rearrange("p (c j) -> p c j", j=8),
            in_=wrt_d.rearrange("(c p) j -> p c j", p=128)),
            writes=["wrtb"], dma="dcp2", label="ld_wrtb")
        sch.merge_writers(["abf", "wrtb"])
        for g in range(NG):
            bank = g % 2
            pdst = psB[:, bank * 512:bank * 512 + 512]
            for u in range(4):
                c = g * 4 + u
                sch.op("pe", lambda e, c=c, u=u, pdst=pdst: e.matmul(
                    pdst[:, u * 128:(u + 1) * 128], abf[:, c * 128:(c + 1) * 128],
                    ident[:], start=True, stop=True),
                    reads=["abf", "ident"], writes=[f"psB{bank}"], label=f"atr{c}")
            sch.op("act", lambda e, g=g, pdst=pdst: e.copy(
                out=atn[:, g * 512:(g + 1) * 512], in_=pdst),
                reads=[f"psB{bank}"], writes=["atn"], label=f"aev{g}")
        def emit_low(c):
            for b in range(NB):
                sch.op("pe", lambda e, c=c, b=b: e.matmul(
                    psA[:, b * 512:(b + 1) * 512], atn[:, c * 128:(c + 1) * 128],
                    xnt_sl(c, b * 512, (b + 1) * 512),
                    start=(c == 0), stop=(c == KC - 1)),
                    reads=["atn", f"xnt{c}.g{b}"], writes=[f"psA{b}"],
                    label=f"low{c}.{b}")

        def emit_str(tb):
            sb_ = (tb % 2) * 2 + 1
            sofs = sb_ * 512
            sch.op("pe", lambda e, tb=tb, sofs=sofs: e.matmul(
                psB[0:8, sofs:sofs + 128], snb[:, (tb % 2) * 8:(tb % 2) * 8 + 8],
                ident[:], start=True, stop=True),
                reads=[f"snb{tb % 2}", "ident"], writes=[f"psB{sb_}"], label=f"str{tb}")
            sch.op("act", lambda e, tb=tb, sofs=sofs: e.copy(
                out=scT[:, tb * 128:(tb + 1) * 128], in_=psB[0:8, sofs:sofs + 128]),
                reads=[f"psB{sb_}"], writes=["scT"], label=f"sev{tb}")

        for tb in range(NT):
            lb = (tb % 2) * 2
            lofs = lb * 512
            sch.op("pe", lambda e, lofs=lofs: e.matmul(
                psB[:, lofs:lofs + 8], ones1[:], brs[:], start=True, stop=False),
                reads=["ones1", "brs"], writes=[f"psB{lb}"], label=f"rbias{tb}")
            for c in range(KC):
                sch.op("pe", lambda e, c=c, tb=tb, lofs=lofs: e.matmul(
                    psB[:, lofs:lofs + 8], xnt_sl(c, tb * 128, (tb + 1) * 128),
                    wrtb[:, c * 8:(c + 1) * 8],
                    start=False, stop=(c == KC - 1)),
                    reads=[f"xnt{c}.g{tb // 4}", "wrtb"], writes=[f"psB{lb}"],
                    label=f"rlog{tb}.{c}")
            if tb >= 1:
                emit_str(tb - 1)
            emit_low(2 * tb)
            emit_low(2 * tb + 1)
            rb = (tb % 2) * 8
            sch.op("act", lambda e, lofs=lofs, rb=rb: e.copy(
                out=rsc[:, rb:rb + 8], in_=psB[:, lofs:lofs + 8]),
                reads=[f"psB{lb}"], writes=[f"rsc{tb % 2}"], label=f"rev{tb}")
            sch.op("dve", lambda e, rb=rb: e.reduce_max(
                rsc[:, rb + 16:rb + 17], rsc[:, rb:rb + 8], axis=AX.X),
                reads=[f"rsc{tb % 2}"], writes=[f"rsc{tb % 2}"], label=f"rmax{tb}")
            sch.op("dve", lambda e, rb=rb: e.tensor_scalar(
                out=rsc[:, rb + 17:rb + 18], in0=rsc[:, rb + 16:rb + 17],
                scalar1=-1.0, scalar2=None, op0=ALU.mult),
                reads=[f"rsc{tb % 2}"], writes=[f"rsc{tb % 2}"], label=f"rneg{tb}")
            sch.op("act", lambda e, rb=rb: e.activation(
                out=rsc[:, rb + 32:rb + 40], in_=rsc[:, rb:rb + 8], func=AF.Exp,
                bias=rsc[:, rb + 17:rb + 18], scale=1.0,
                accum_out=rsc[:, rb + 18:rb + 19]),
                reads=[f"rsc{tb % 2}"], writes=[f"rsc{tb % 2}"], label=f"rexp{tb}")
            sch.op("dve", lambda e, rb=rb: e.reciprocal(
                out=rsc[:, rb + 19:rb + 20], in_=rsc[:, rb + 18:rb + 19]),
                reads=[f"rsc{tb % 2}"], writes=[f"rsc{tb % 2}"], label=f"rrec{tb}")
            sch.op("dve", lambda e, rb=rb, tb=tb: e.tensor_scalar(
                out=snb[:, (tb % 2) * 8:(tb % 2) * 8 + 8], in0=rsc[:, rb + 32:rb + 40],
                scalar1=rsc[:, rb + 19:rb + 20], scalar2=16.0,
                op0=ALU.mult, op1=ALU.mult),
                reads=[f"rsc{tb % 2}"], writes=[f"snb{tb % 2}"], label=f"rsc{tb}")
        emit_str(NT - 1)
        for b in range(NB):
            sch.op("act", lambda e, b=b: e.copy(
                out=lowbf[:, b * 512:(b + 1) * 512], in_=psA[:, b * 512:(b + 1) * 512]),
                reads=[f"psA{b}"], writes=["lowbf"], label=f"lowev{b}")
        for b in range(NB):
            sch.op("pe", lambda e, b=b: e.matmul(
                psA[:, b * 512:(b + 1) * 512], emat[:], scT[:, b * 512:(b + 1) * 512],
                start=True, stop=True),
                reads=["emat", "scT"], writes=[f"psA{b}"], label=f"bc{b}")
            sch.op("dve", lambda e, b=b: e.tensor_tensor(
                out=lst[:, b * 512:(b + 1) * 512], in0=lowbf[:, b * 512:(b + 1) * 512],
                in1=psA[:, b * 512:(b + 1) * 512], op=ALU.mult),
                reads=["lowbf", f"psA{b}"], writes=["lst"], label=f"lst{b}")

        qb = [TB.take("qb0", 0, S), TB.take("qb1", S, 2 * S)]
        kb = [TB.take("kb0", 2 * S, 3 * S), TB.take("kb1", 3 * S, 4 * S)]
        TF.take("vbuf0", 2048, 3072)
        TF.take("vbuf1", 1024, 2048)
        vbuf = [tf32[:, 2048:3072].bitcast(BF16),
                tf32[:, 1024:2048].bitcast(BF16)]
        PRB0 = 4 * S
        prb = [TB.take(f"prb{c}", PRB0 + c * 512, PRB0 + (c + 1) * 512)
               for c in range(NT)]
        TF.take(["rcpS0", "rcpS1"], 0, 1024)
        rcpS = [tf32[:, 0:512], tf32[:, 512:1024]]

        def emit_qkv_load(h):
            if h >= KC:
                return
            par = h % 2
            sch.op("sp", lambda e, h=h, par=par: e.dma_start(
                out=qb[par], in_=qts_d[h * 128:(h + 1) * 128, :]),
                reads=[f"qts{h}"], writes=[f"qb{par}"], dma=f"dq{par}", label=f"qld{h}")
            sch.op("sp", lambda e, h=h, par=par: e.dma_start(
                out=kb[par], in_=kts_d[h * 128:(h + 1) * 128, :]),
                reads=[f"kts{h}"], writes=[f"kb{par}"], dma=f"dq{par}", label=f"kld{h}")
            sch.merge_writers([f"qb{par}", f"kb{par}"])

        def emit_v_load(h):
            if h >= KC:
                return
            vp = h % 2
            sch.op("sp", lambda e, h=h, vp=vp: e.dma_start(
                out=vbuf[vp], in_=vs_d[h]),
                reads=[f"vs{h}.g{g}" for g in range(NT // 4)],
                writes=[f"vbuf{vp}"], dma="dv0", label=f"vld{h}")

        emit_qkv_load(0)
        emit_qkv_load(1)
        emit_v_load(0)
        emit_v_load(1)

        for h in range(KC):
            par = h % 2
            vp = h % 2
            for p in range(NB):
                sch.alias(f"attnT{h}.p{p}", [f"xnt{h}.g{p}"])
            for p in range(NB):
                pg = p % 2
                ps, psn = (psA, "psA") if pg == 0 else (psB, "psB")
                qlo = p * 512
                nchunks = 4 * (p + 1)

                def emit_sc(c, p=p, qlo=qlo, ps=ps, psn=psn):
                    sb = c % 2
                    qoff = max(0, c * 128 - qlo)
                    sch.op("pe", lambda e, c=c, sb=sb, qoff=qoff, qlo=qlo,
                           par=par, ps=ps: e.matmul(
                        ps[:, sb * 512 + qoff: (sb + 1) * 512],
                        kb[par][:, c * 128:(c + 1) * 128],
                        qb[par][:, qlo + qoff: qlo + 512],
                        start=True, stop=True),
                        reads=[f"kb{par}", f"qb{par}"], writes=[f"{psn}{sb}"],
                        label=f"sc{h}.{p}.{c}")
                    sch.op("act", lambda e, c=c, sb=sb, qoff=qoff, ps=ps: e.activation(
                        out=prb[c][:, qoff:512],
                        in_=ps[:, sb * 512 + qoff: (sb + 1) * 512],
                        func=AF.Exp, bias=kbias[:, 1:2], scale=SC),
                        reads=[f"{psn}{sb}", "kbias"], writes=[f"prb{c}"],
                        label=f"ex{h}.{p}.{c}")
                    if c * 128 >= qlo:
                        sch.op("pool", lambda e, c=c, qoff=qoff: e.tensor_tensor(
                            out=prb[c][:, qoff:qoff + 128],
                            in0=prb[c][:, qoff:qoff + 128],
                            in1=maskut[:], op=ALU.mult),
                            reads=[f"prb{c}", "maskut"], writes=[f"prb{c}"],
                            label=f"mk{h}.{p}.{c}")

                def emit_rspv(c, p=p, qlo=qlo, nchunks=nchunks, ps=ps, psn=psn):
                    qoff = max(0, c * 128 - qlo)
                    sch.op("pe", lambda e, c=c, qoff=qoff, ps=ps,
                           nchunks=nchunks: e.matmul(
                        ps[:, 1536 + qoff:2048],
                        onesd[:], prb[c][:, qoff:512],
                        start=(c == 0), stop=(c == nchunks - 1)),
                        reads=["onesd", f"prb{c}"], writes=[f"{psn}3"],
                        label=f"rs{h}.{p}.{c}")
                    sch.op("pe", lambda e, c=c, qoff=qoff, ps=ps, vp=vp,
                           nchunks=nchunks: e.matmul(
                        ps[:, 1024 + qoff:1536],
                        vbuf[vp][:, c * 128:(c + 1) * 128],
                        prb[c][:, qoff:512],
                        start=(c == 0), stop=(c == nchunks - 1)),
                        reads=[f"vbuf{vp}", f"prb{c}"], writes=[f"{psn}2"],
                        label=f"pv{h}.{p}.{c}")

                for c in range(nchunks):
                    emit_sc(c)
                    if c >= 2:
                        emit_rspv(c - 2)
                emit_rspv(nchunks - 2)
                emit_rspv(nchunks - 1)

                sch.op("dve", lambda e, ps=ps, pg=pg: e.reciprocal(
                    out=rcpS[pg], in_=ps[:, 1536:2048]),
                    reads=[f"{psn}3"], writes=[f"rcpS{pg}"], label=f"rec{h}.{p}")
                sch.op("dve", lambda e, h=h, qlo=qlo, ps=ps, pg=pg: e.tensor_tensor(
                    out=xnt_sl(h, qlo, qlo + 512), in0=ps[:, 1024:1536],
                    in1=rcpS[pg], op=ALU.mult),
                    reads=[f"{psn}2", f"rcpS{pg}"],
                    writes=[f"attnT{h}.p{p}"], label=f"aev{h}.{p}")
            emit_qkv_load(h + 2)
            emit_v_load(h + 2)

        sT_t = [TB.take("sT0", 0, S), TB.take("sT1", S + D, 2 * S + D)]
        bst = TB.take("bst", S, S + D)
        xcol = TF.take("xcol", 0, S)
        ocol = TF.take("ocol", S, 2 * S)
        sch.op("pool", lambda e: e.dma_start(out=bst, in_=bp_d),
               writes=["bst"], dma="dbst", label="ld_bst")

        def emit_wo_load(ob):
            if ob >= KC:
                return
            buf = ob % 2
            for g in range(NGRP):
                def ld(e, ob=ob, buf=buf, g=g):
                    src = wo_d.rearrange("(c p) n -> p c n", p=128)[
                        :, g * PER:(g + 1) * PER, ob * 128:(ob + 1) * 128]
                    dst = wpan[buf][:].rearrange("p (c j) -> p c j", j=128)[
                        :, g * PER:(g + 1) * PER, :]
                    return e.dma_start(out=dst, in_=src)
                sch.op("pool", ld, writes=[f"wpan{buf}.g{g}"], dma=f"dw{buf}",
                       label=f"wold{ob}.{g}")
            sch.merge_writers([f"wpan{buf}.g{g}" for g in range(NGRP)])

        def emit_xcl(ob):
            def xld(e, ob=ob):
                src = x_d.rearrange("(a p) n -> p a n", p=128)[
                    :, :, ob * 128:(ob + 1) * 128]
                return e.dma_start(out=xcol.rearrange("p (a n) -> p a n", n=128),
                                   in_=src)
            sch.op("sp", xld, reads=[], writes=["xcol"], dma="dxc", label=f"xcl{ob}")

        def emit_tail(ob):
            sT = sT_t[ob % 2]
            sn = f"sT{ob % 2}"
            for g in range(NT // 4):
                vb = g % 2
                pdst = psB[:, vb * 512:vb * 512 + 512]
                for u in range(4):
                    t = g * 4 + u
                    sch.op("pe", lambda e, t=t, u=u, pdst=pdst, sT=sT: e.matmul(
                        pdst[:, u * 128:(u + 1) * 128], sT[:, t * 128:(t + 1) * 128],
                        ident[:], start=True, stop=True),
                        reads=[sn, "ident"], writes=[f"psB{vb}"],
                        label=f"ftr{ob}.{t}")
                sch.op("dve", lambda e, g=g, pdst=pdst: e.tensor_tensor(
                    out=ocol[:, g * 512:(g + 1) * 512], in0=pdst,
                    in1=xcol[:, g * 512:(g + 1) * 512], op=ALU.add),
                    reads=[f"psB{vb}", "xcol"], writes=["ocol"], label=f"fadd{ob}.{g}")
            def ost(e, ob=ob):
                dst = out_d.rearrange("(a p) n -> p a n", p=128)[
                    :, :, ob * 128:(ob + 1) * 128]
                return e.dma_start(out=dst,
                                   in_=ocol.rearrange("p (a n) -> p a n", n=128))
            sch.op("sp", ost, reads=["ocol"], writes=[f"outc{ob}"], dma="dou",
                   label=f"ost{ob}")

        emit_wo_load(0)
        emit_wo_load(1)
        emit_xcl(0)

        for ob in range(KC):
            buf = ob % 2
            for c in range(KC):
                for b in range(NB):
                    sch.op("pe", lambda e, c=c, b=b, buf=buf: e.matmul(
                        psA[:, b * 512:(b + 1) * 512],
                        wpan[buf][:, c * 128:(c + 1) * 128],
                        xnt_sl(c, b * 512, (b + 1) * 512),
                        start=(c == 0), stop=False),
                        reads=[f"wpan{buf}.g{c // PER}", f"attnT{c}.p{b}"],
                        writes=[f"psA{b}"], label=f"op{ob}.{c}.{b}")
            emit_wo_load(ob + 2)
            for b in range(NB):
                sch.op("pe", lambda e, b=b, ob=ob: e.matmul(
                    psA[:, b * 512:(b + 1) * 512], bst[:, ob * 128:(ob + 1) * 128],
                    lst[:, b * 512:(b + 1) * 512], start=False, stop=True),
                    reads=["bst", "lst"], writes=[f"psA{b}"], label=f"fu{ob}.{b}")
            if ob > 0:
                emit_tail(ob - 1)
                emit_xcl(ob)
            sch.op("act", lambda e, ob=ob: e.copy(out=sT_t[ob % 2], in_=psA[:, 0:S]),
                   reads=[f"psA{b}" for b in range(NB)], writes=[f"sT{ob % 2}"],
                   label=f"sev{ob}")
        emit_tail(KC - 1)

        finals = dict(sch.cnt)

        @block.tensor
        def _(eng):
            sch.emit("pe", eng, sems)

        @block.scalar
        def _(eng):
            sch.emit("act", eng, sems)

        @block.vector
        def _(eng):
            sch.emit("dve", eng, sems)

        @block.gpsimd
        def _(eng):
            sch.emit("pool", eng, sems)

        @block.sync
        def _(eng):
            sch.emit("sp", eng, sems)
            for s in DMA_SEMS + ("pe", "act", "dve", "pool"):
                if s in finals:
                    eng.wait_ge(sems[s], finals[s])

    return nc


def make_consts(S, D):
    HD = 128
    c = {}
    c["ident_hbm"] = np.eye(128, dtype=ml_dtypes.bfloat16)
    i, j = np.mgrid[0:128, 0:128]
    c["maskut_hbm"] = (j >= i).astype(ml_dtypes.bfloat16)
    c["onesd_hbm"] = np.ones((128, 128), dtype=ml_dtypes.bfloat16)
    c["ones1_hbm"] = np.ones((1, 128), dtype=ml_dtypes.bfloat16)
    li = np.arange(128)
    e = np.zeros((8, 128), dtype=np.float32)
    e[li % 8, li] = 1.0
    c["emat_hbm"] = e.astype(ml_dtypes.bfloat16)
    inv = 1.0 / 10000.0 ** (np.arange(0, HD, 2)[: HD // 2].astype(np.float32) / HD)
    t = np.arange(S)
    f = np.outer(t, inv).astype(np.float32)
    cos = np.cos(f)
    sin = np.sin(f)
    p = np.arange(128)
    ctab = cos[:, p // 2].T
    sgn = np.where(p % 2 == 0, -1.0, 1.0).astype(np.float32)
    stab = (sin[:, p // 2] * sgn[None, :]).T
    c["ctab_hbm"] = np.ascontiguousarray(ctab).astype(ml_dtypes.bfloat16)
    c["stab_hbm"] = np.ascontiguousarray(stab).astype(ml_dtypes.bfloat16)
    return c


def make_in_maps(x, wq, wk, wv, wo, A, B, Wr, br, S, D):
    shared = {
        "wq": np.ascontiguousarray(wq, dtype=np.float32),
        "wk": np.ascontiguousarray(wk, dtype=np.float32),
        "wv": np.ascontiguousarray(wv, dtype=np.float32),
        "wo": np.ascontiguousarray(wo, dtype=np.float32),
        "ap": np.ascontiguousarray(
            np.asarray(A, dtype=np.float32).transpose(1, 0, 2).reshape(128, D)),
        "bp": np.ascontiguousarray(
            np.asarray(B, dtype=np.float32).transpose(2, 0, 1).reshape(128, D)),
        "wrt": np.ascontiguousarray(np.asarray(Wr, dtype=np.float32).T),
        "brs_hbm": np.asarray(br, dtype=np.float32).reshape(1, 8).astype(ml_dtypes.bfloat16),
    }
    shared.update(make_consts(S, D))
    maps = []
    for b in range(x.shape[0]):
        m = dict(shared)
        m["x"] = np.ascontiguousarray(x[b], dtype=np.float32)
        maps.append(m)
    return maps


def kernel(L0_attn_x, L0_attn_wq, L0_attn_wk, L0_attn_wv, L0_attn_wo,
           L0_ffn_A, L0_ffn_B, L0_ffn_Wr, L0_ffn_br):
    from concourse.bass_utils import run_bass_kernel_spmd
    S, D = 2048, 4096
    x = np.asarray(L0_attn_x, dtype=np.float32)
    nc = build_program(S, D)
    maps = make_in_maps(x, L0_attn_wq, L0_attn_wk, L0_attn_wv, L0_attn_wo,
                        L0_ffn_A, L0_ffn_B, L0_ffn_Wr, L0_ffn_br, S, D)
    res = run_bass_kernel_spmd(nc, maps, core_ids=list(range(8)))
    return np.stack([r["out"] for r in res.results], axis=0)
